# version 13
# baseline (speedup 1.0000x reference)
import numpy as np
import ml_dtypes
import concourse.bass as bass
import concourse.mybir as mybir
from concourse.bass_utils import run_bass_kernel_spmd

F32 = mybir.dt.float32
BF16 = mybir.dt.bfloat16
I32 = mybir.dt.int32
AF = mybir.ActivationFunctionType
ALU = mybir.AluOpType

NCORES = 8
S_FULL, D, HID = 8192, 2048, 5632
T = S_FULL // NCORES
NTT = T // 128
NDC = D // 128
NIC = D // 128
NIG = 4
NHC = HID // 128
EPS = 1e-5
ENGINES = ("sync", "gpsimd", "tensor", "vector", "scalar")


def _run(e, lst):
    for f in lst:
        f(e)


class MK:
    PHASES = ("a0", "ak", "av", "cc", "b1a", "b1b", "b2c", "d1", "d2", "d3")
    B2C_TTS = NTT
    B2C_TAIL = True

    def __init__(self, phases=None, debug_out=None):
        self.phases = self.PHASES if phases is None else phases
        self.debug_out = debug_out
        nc = bass.Bass("TRN2", target_bir_lowering=False, debug=False)
        self.nc = nc
        self.ops = {e: [] for e in ENGINES}
        self.sems = {}
        self.cnt = {}
        self.guards = []
        self.live = {}
        for s in ("c", "x00", "x01", "xw0", "xw1", "xr0", "xr1", "y0", "y1",
                  "wa0", "wa1", "wq0", "wq1", "wq2", "wq3", "wo", "wg0", "wg1",
                  "wu0", "wu1", "wd0", "wd1", "gb",
                  "cc", "t", "v", "s"):
            g = nc.semaphore(s)
            self.sems[s] = g.__enter__()
            self.cnt[s] = 0
            self.guards.append(g)
        self.declare_dram()
        self.alloc_global()
        self.walk()
        self.finalize()

    def emit(self, eng, f):
        self.ops[eng].append(f)

    def w(self, eng, sname, val):
        if val is None or val <= 0:
            return
        self.emit(eng, lambda e, s=self.sems[sname], v=val: e.wait_ge(s, v))

    def dma(self, eng, sname, out_ap, in_ap):
        self.cnt[sname] += 16
        self.emit(eng, lambda e, o=out_ap, i=in_ap, s=self.sems[sname]:
                  e.dma_start(out=o, in_=i).then_inc(s, 16))
        return self.cnt[sname]

    def op(self, eng, fname, *args, inc=None, **kw):
        semv = None
        s = None
        if inc is not None:
            self.cnt[inc] += 1
            semv = self.cnt[inc]
            s = self.sems[inc]
        def f(e, fn=fname, a=args, k=kw, s=s):
            ins = getattr(e, fn)(*a, **k)
            if s is not None:
                ins.then_inc(s, 1)
        self.emit(eng, f)
        return semv

    def tmark(self):
        self.cnt["t"] += 1
        v = self.cnt["t"]
        self.emit("tensor", lambda e, s=self.sems["t"]: e.drain().then_inc(s, 1))
        return v

    def vsync(self, v):
        self.w("vector", "v", v)

    def sb(self, name, shape, dt, off):
        return self.nc.alloc_sbuf_tensor_at(name, shape, dt, offset=off)

    def ps(self, name, shape, dt=F32):
        g = self.nc.psum_tensor(name, shape, dt)
        t = g.__enter__()
        self.guards.append(g)
        self.live[name] = g
        return t

    def free(self, *names):
        for n in names:
            self.live.pop(n).__exit__(None, None, None)

    def declare_dram(self):
        nc = self.nc
        dt = nc.dram_tensor
        self.d_x = dt("x", [T, D], F32, kind="ExternalInput")
        self.d_wq = dt("wq", [NIC, 128, NDC, 128], F32, kind="ExternalInput")
        self.d_wk = dt("wk", [NIG, 128, NDC, 512], F32, kind="ExternalInput")
        self.d_wv = dt("wv", [NIG, 128, NDC, 512], F32, kind="ExternalInput")
        self.d_wo = dt("wo", [D, D], F32, kind="ExternalInput")
        self.d_bo = dt("bo", [1, D], F32, kind="ExternalInput")
        self.d_wg = dt("wg", [NHC, 128, NDC, 128], I32, kind="ExternalInput")
        self.d_wu = dt("wu", [NHC, 128, NDC, 128], I32, kind="ExternalInput")
        self.d_wd = dt("wd", [NDC, 128, NHC, 128], I32, kind="ExternalInput")
        self.d_sg = dt("sg", [128, NHC], F32, kind="ExternalInput")
        self.d_su = dt("su", [128, NHC], F32, kind="ExternalInput")
        self.d_sd = dt("sd", [128, NDC], F32, kind="ExternalInput")
        self.d_cst = dt("cst", [128, 512], BF16, kind="ExternalInput")
        self.d_y = dt("y", [T, D], F32, kind="ExternalOutput")
        self.d_x1 = dt("x1d", [T, D], F32)
        self.d_cci = dt("cc_in", [128, 1040], F32)
        self.d_cco = dt("cc_out", [128, 1040], F32)

    def alloc_global(self):
        B = 17408
        m = {}
        off = B
        for name, nbytes in [("cst_sb", 1024), ("tiny", 128), ("sg_sb", 192),
                             ("su_sb", 192), ("sd_sb", 64), ("bo_sb", 4096),
                             ("scr", 4096), ("scr2", 4096), ("lntt0", 4096), ("lntt1", 4096),
                             ("x1_0", 8192), ("x1_1", 8192), ("kvblk", 4096),
                             ("ksrep", 4096), ("ln_fm", 32768), ("x0", 8192),
                             ("x1s", 8192), ("out0", 8192), ("out1", 8192)]:
            m[name] = off
            off += nbytes
        self.P = off
        self.m = m
        assert self.P + 8192 + 90112 <= 229376, self.P
        self.ln_fm = self.sb("ln_fm", [128, NDC * 1024], BF16, m["ln_fm"])
        self.x_sb = [self.sb("xs0", [128, D], F32, m["x0"]),
                     self.sb("xs1", [128, D], F32, m["x1s"])]
        self.x1_sb = [self.sb("x1_0", [128, D], F32, m["x1_0"]),
                      self.sb("x1_1", [128, D], F32, m["x1_1"])]
        self.lntt = [self.sb("lntt0", [128, D], BF16, m["lntt0"]),
                     self.sb("lntt1", [128, D], BF16, m["lntt1"])]
        self.scrV = self.sb("scrV", [128, D], BF16, m["scr"])
        self.scrS = self.sb("scrS", [128, D], BF16, m["scr2"])
        self.cst = self.sb("cst_sb", [128, 512], BF16, m["cst_sb"])
        self.tiny = self.sb("tiny", [128, 20], F32, m["tiny"])
        self.kvblk = self.sb("kvblk", [128, 2048], BF16, m["kvblk"])
        self.ksrep = self.sb("ksrep", [128, 2048], BF16, m["ksrep"])
        self.bo_sb = self.sb("bo_sb", [1, D], BF16, m["bo_sb"])
        self.sg_sb = self.sb("sg_sb", [128, NHC], F32, m["sg_sb"])
        self.su_sb = self.sb("su_sb", [128, NHC], F32, m["su_sb"])
        self.sd_sb = self.sb("sd_sb", [128, NDC], F32, m["sd_sb"])
        self.ident = self.cst[:, 0:128]
        self.blockmask = self.cst[:, 128:256]
        self.ones_col = self.cst[:, 256:257]
        self.ones_row = self.cst[0:1, 258:386]

    def tv(self, i, slot):
        c = i * 2 + slot
        return self.tiny[:, c:c + 1]

    def ln_stats(self, src_ap, slot, psum=False):
        sumv, sumsq = self.tv(0, slot), self.tv(1, slot)
        mneg, ve, m2, rc = (self.tv(2, slot), self.tv(3, slot),
                            self.tv(4, slot), self.tv(7, slot))
        n_inv = 1.0 / D
        v0 = self.op("vector", "tensor_scalar", self.scrV[:, :], src_ap, 1.0, 0.0,
                     ALU.mult, ALU.add, accum_out=sumv, inc="v")
        if psum:
            self.w("scalar", "v", v0)
        s_sq = self.op("scalar", "activation", self.scrS[:, :], src_ap, AF.Square,
                       accum_out=sumsq, inc="s")
        self.vsync(v0)
        v1 = self.op("vector", "tensor_scalar", mneg, sumv, -n_inv, 0.0,
                     ALU.mult, ALU.add, inc="v")
        self.vsync(v1)
        v2 = self.op("vector", "tensor_mul", m2, mneg, mneg, inc="v")
        self.w("vector", "s", s_sq)
        v3 = self.op("vector", "tensor_scalar", ve, sumsq, n_inv, EPS,
                     ALU.mult, ALU.add, inc="v")
        self.vsync(v3)
        self.vsync(v2)
        v4 = self.op("vector", "tensor_sub", ve, ve, m2, inc="v")
        self.vsync(v4)
        return v4, s_sq

    def ln_rs_newton(self, slot, v_ve):
        ve, y, t1, t2 = (self.tv(3, slot), self.tv(5, slot),
                         self.tv(4, slot), self.tv(7, slot))
        nm = self.tv(6, slot)
        self.vsync(v_ve)
        vv = self.op("vector", "reciprocal", y, ve, inc="v")
        self.vsync(vv)
        vv = self.op("vector", "tensor_scalar_min", y, y, 1.0, inc="v")
        for _ in range(10):
            self.vsync(vv)
            vv = self.op("vector", "tensor_mul", t1, y, y, inc="v")
            self.vsync(vv)
            vv = self.op("vector", "tensor_mul", t2, t1, ve, inc="v")
            self.vsync(vv)
            vv = self.op("vector", "tensor_scalar", t1, t2, -0.5, 1.5,
                         ALU.mult, ALU.add, inc="v")
            self.vsync(vv)
            vv = self.op("vector", "tensor_mul", y, y, t1, inc="v")
        self.vsync(vv)
        v_cnt = self.op("vector", "tensor_mul", nm, self.tv(2, slot), y, inc="v")
        self.vsync(v_cnt)
        return y, nm, v_cnt

    def ln_rs(self, slot, v_ve):
        r, nm, lnv = self.tv(5, slot), self.tv(6, slot), self.tv(7, slot)
        self.w("scalar", "v", v_ve)
        s0 = self.op("scalar", "activation", lnv, self.tv(3, slot), AF.Ln, inc="s")
        self.w("scalar", "s", s0)
        s1 = self.op("scalar", "activation", r, lnv, AF.Exp, scale=-0.5, inc="s")
        self.w("scalar", "s", s1)
        s_cnt = self.op("scalar", "mul", nm, self.tv(2, slot), r, inc="s")
        self.w("scalar", "s", s_cnt)
        return r, nm, s_cnt

    def transpose_batches(self, pTile, src_tile, tt, free_half, lnfm3):
        for half_i in range(2):
            hp = self.tp_batch % 2
            self.w("tensor", "v", free_half[hp])
            for j in range(8):
                dc = half_i * 8 + j
                self.op("tensor", "matmul",
                        pTile[:, hp * 1024 + j * 128:hp * 1024 + (j + 1) * 128],
                        src_tile[:, dc * 128:(dc + 1) * 128],
                        self.ident, start=(j % 4 == 0), stop=True,
                        skip_group_check=True)
            tb = self.tmark()
            self.w("vector", "t", tb)
            pv = pTile[:, hp * 1024:(hp + 1) * 1024].rearrange(
                "p (c u) -> p c u", c=8)
            free_half[hp] = self.op(
                "vector", "tensor_copy",
                lnfm3[:, half_i * 8:half_i * 8 + 8, tt * 128:(tt + 1) * 128],
                pv, inc="v")
            self.tp_batch += 1

    def walk(self):
        self.consts()
        table = [("a0", self.phase_a0), ("ak", self.phase_ak),
                 ("av", self.phase_av), ("cc", self.collective),
                 ("b1a", self.phase_b1a), ("b1b", self.phase_b1b),
                 ("b2c", self.phase_b2c), ("d1", self.phase_d1),
                 ("d2", self.phase_d2), ("d3", self.phase_d3)]
        for name, fn in table:
            if name in self.phases:
                fn()
        if self.debug_out is not None:
            self.debug_out(self)

    def consts(self):
        self.dma("sync", "c", self.cst[:, :], self.d_cst[:, :])
        self.dma("sync", "c", self.sg_sb[:, :], self.d_sg[:, :])
        self.dma("sync", "c", self.su_sb[:, :], self.d_su[:, :])
        self.dma("sync", "c", self.sd_sb[:, :], self.d_sd[:, :])
        self.dma("gpsimd", "gb", self.bo_sb[:, :], self.d_bo[:, :])

    def phase_a0(self):
        self.pT = [self.ps(f"pT{i}", [128, 2048], F32) for i in range(2)]
        self.a0_x_done = []
        self.v_pT = [0, 0]
        self.s_apply = []
        lnfm3 = self.ln_fm[:, :].rearrange("p (c u) -> p c u", c=NDC)
        self.w("tensor", "c", 64)
        for tt in range(NTT):
            slot = tt % 2
            if tt >= 2:
                vv, ss = self.a0_x_done[tt - 2]
                self.w("sync", "v", vv)
                self.w("sync", "s", ss)
            vx = self.dma("sync", f"x0{slot}", self.x_sb[slot][:, :],
                          self.d_x[tt * 128:(tt + 1) * 128, :])
            self.w("vector", f"x0{slot}", vx)
            self.w("scalar", f"x0{slot}", vx)
            v_rc, _ = self.ln_stats(self.x_sb[slot][:, :], slot)
            v_stats = self.cnt["v"]
            r, nm, _ = self.ln_rs(slot, v_rc)
            s_ap = self.op("scalar", "activation", self.lntt[slot][:, :],
                           self.x_sb[slot][:, :], AF.Identity, bias=nm, scale=r,
                           inc="s")
            self.s_apply.append(s_ap)
            self.a0_x_done.append((v_stats, s_ap))
            self.w("tensor", "s", s_ap)
            self.w("tensor", "v", self.v_pT[slot])
            for dc in range(NDC):
                self.op("tensor", "matmul",
                        self.pT[slot][:, dc * 128:(dc + 1) * 128],
                        self.lntt[slot][:, dc * 128:(dc + 1) * 128],
                        self.ident, start=(dc % 4 == 0), stop=True,
                        skip_group_check=True)
            tb = self.tmark()
            self.w("vector", "t", tb)
            pv = self.pT[slot][:, :].rearrange("p (c u) -> p c u", c=NDC)
            self.v_pT[slot] = self.op(
                "vector", "tensor_copy",
                lnfm3[:, :, tt * 128:(tt + 1) * 128], pv, inc="v")
        self.a0_v_end = self.cnt["v"]
        self.a0_s_end = self.cnt["s"]

    def phase_ak(self):
        self.free("pT1", "pT0")
        P = self.P
        self.wkv_sb = [self.sb("wkv0", [128, NDC * 512], BF16, P),
                       self.sb("wkv1", [128, NDC * 512], BF16, P + 16384)]
        self.k_tm = self.sb("k_tm", [128, NTT * 2048], BF16, P + 32768)
        self.pa = self.ps("pa", [128, 2048])
        self.t_wslot = [0, 0]
        self.v_pa = [0, 0]
        pidx = 0
        self.w("tensor", "v", self.a0_v_end)
        for ig in range(NIG):
            slot = ig % 2
            self.w("gpsimd", "t", self.t_wslot[slot])
            ld = self.dma("gpsimd", f"wa{slot}",
                          self.wkv_sb[slot][:, :].rearrange("p (c i) -> p c i", c=NDC),
                          self.d_wk[ig, :, :, :])
            self.w("tensor", f"wa{slot}", ld)
            for tp in range(NTT // 2):
                ph = pidx % 2
                self.w("tensor", "v", self.v_pa[ph])
                for j in range(2):
                    tt = tp * 2 + j
                    col = ph * 1024 + j * 512
                    for dc in range(NDC):
                        self.op("tensor", "matmul", self.pa[:, col:col + 512],
                                self.ln_fm[:, dc * 1024 + tt * 128:dc * 1024 + (tt + 1) * 128],
                                self.wkv_sb[slot][:, dc * 512:(dc + 1) * 512],
                                start=(dc == 0), stop=(dc == NDC - 1))
                tm = self.tmark()
                self.w("vector", "t", tm)
                tt0 = tp * 2
                ktm3 = self.k_tm[:, :].rearrange("p (a u) -> p a u", a=NTT)
                pa3 = self.pa[:, ph * 1024:(ph + 1) * 1024].rearrange(
                    "p (a u) -> p a u", a=2)
                self.v_pa[ph] = self.op(
                    "vector", "tensor_scalar_max",
                    ktm3[:, tt0:tt0 + 2, ig * 512:(ig + 1) * 512],
                    pa3, 0.0, inc="v")
                pidx += 1
            self.t_wslot[slot] = self.cnt["t"]
        self.ak_idx = pidx

    def phase_av(self):
        P = self.P
        self.v_tt = [self.sb("vtt0", [128, 1024], BF16, P + 65536),
                     self.sb("vtt1", [128, 1024], BF16, P + 67584)]
        self.kvks = self.sb("kvks", [128, 1040], F32, P + 69632)
        self.kvp = self.ps("kvp", [128, 1024])
        self.ksp = self.ps("ksp", [128, 1024])
        pidx = self.ak_idx
        vslot_free = [0, 0]
        self.w("tensor", "c", 64)
        kv_started = set()
        ks_started = set()
        for ig in range(NIG):
            slot = ig % 2
            self.w("gpsimd", "t", self.t_wslot[slot])
            ld = self.dma("gpsimd", f"wa{slot}",
                          self.wkv_sb[slot][:, :].rearrange("p (c i) -> p c i", c=NDC),
                          self.d_wv[ig, :, :, :])
            self.w("tensor", f"wa{slot}", ld)
            for tp in range(NTT // 2):
                ph = pidx % 2
                vs = pidx % 2
                self.w("tensor", "v", self.v_pa[ph])
                for j in range(2):
                    tt = tp * 2 + j
                    col = ph * 1024 + j * 512
                    for dc in range(NDC):
                        self.op("tensor", "matmul", self.pa[:, col:col + 512],
                                self.ln_fm[:, dc * 1024 + tt * 128:dc * 1024 + (tt + 1) * 128],
                                self.wkv_sb[slot][:, dc * 512:(dc + 1) * 512],
                                start=(dc == 0), stop=(dc == NDC - 1))
                tm = self.tmark()
                self.w("vector", "t", tm)
                self.w("vector", "t", vslot_free[vs])
                v_ev = self.op("vector", "tensor_copy", self.v_tt[vs][:, :],
                               self.pa[:, ph * 1024:(ph + 1) * 1024], inc="v")
                self.v_pa[ph] = v_ev
                self.w("tensor", "v", v_ev)
                for j in range(2):
                    tt = tp * 2 + j
                    for hl in range(8):
                        h = ig * 8 + hl
                        po = 64 * (h % 2)
                        kslice = self.k_tm[:, tt * 2048 + h * 64:tt * 2048 + (h + 1) * 64]
                        kvkey = ((h // 2) // 8, h % 2)
                        self.op("tensor", "matmul",
                                self.kvp[po:po + 64, (h // 2) * 64:(h // 2 + 1) * 64],
                                kslice, self.v_tt[vs][:, j * 512 + hl * 64:j * 512 + (hl + 1) * 64],
                                start=(kvkey not in kv_started),
                                stop=(tt == NTT - 1), skip_group_check=True)
                        kv_started.add(kvkey)
                        kskey = ((h // 2) // 8, h % 2)
                        self.op("tensor", "matmul",
                                self.ksp[po:po + 64, (h // 2) * 64:(h // 2 + 1) * 64],
                                kslice, self.cst[:, 258:322],
                                start=(kskey not in ks_started),
                                stop=(tt == NTT - 1), skip_group_check=True)
                        ks_started.add(kskey)
                vslot_free[vs] = self.tmark()
                pidx += 1
            self.t_wslot[slot] = self.cnt["t"]
        self.t_kv_done = self.cnt["t"]

    def collective(self):
        self.w("vector", "t", self.t_kv_done)
        self.op("vector", "tensor_copy", self.kvks[:, 0:1024], self.kvp[:, :])
        v_kv = self.op("vector", "tensor_copy",
                       self.kvks[:, 1024:1040].rearrange("p (c u) -> p c u", c=16),
                       self.ksp[:, :].rearrange("p (c u) -> p c u", c=16)[:, :, 0:1],
                       inc="v")
        self.v_kvks_evac = v_kv
        self.w("gpsimd", "v", v_kv)
        g1 = self.dma("gpsimd", "gb", self.d_cci[:, :], self.kvks[:, :])
        self.w("gpsimd", "gb", g1)
        self.cnt["cc"] += 1
        self.emit("gpsimd", lambda e, ci=self.d_cci, co=self.d_cco,
                  s=self.sems["cc"]:
                  e.collective_compute(
                      "AllReduce", ALU.add,
                      replica_groups=[list(range(NCORES))],
                      ins=[ci.ap().opt()], outs=[co.ap().opt()]).then_inc(s, 1))

    def collective_finish(self):
        kvks2 = self.sb("kvks2", [128, 1040], F32, self.m["x1_0"])
        self.kvks2 = kvks2
        self.w("gpsimd", "cc", self.cnt["cc"])
        g2 = self.dma("gpsimd", "gb", kvks2[:, :], self.d_cco[:, :])
        v_ms = self.op("vector", "memset", self.kvblk[:, :], 0.0, inc="v")
        self.w("vector", "c", 64)
        self.w("vector", "gb", g2)
        self.vsync(v_ms)
        for c in range(16):
            self.op("vector", "tensor_copy",
                    self.kvblk[0:64, c * 128:c * 128 + 64],
                    self.kvks2[0:64, c * 64:(c + 1) * 64])
            self.op("vector", "tensor_copy",
                    self.kvblk[64:128, c * 128 + 64:(c + 1) * 128],
                    self.kvks2[64:128, c * 64:(c + 1) * 64])
            self.op("vector", "tensor_scalar_mul",
                    self.ksrep[:, c * 128:(c + 1) * 128],
                    self.blockmask, self.kvks2[:, 1024 + c:1025 + c],
                    inc="v" if c == 15 else None)
        self.v_kvblk = self.cnt["v"]

    def phase_b1a(self):
        self.free("ksp", "kvp", "pa")
        m, P = self.m, self.P
        self.wq_sb = [self.sb("wqs0", [128, 2048], BF16, m["out0"]),
                      self.sb("wqs1", [128, 2048], BF16, m["out0"] + 4096),
                      self.sb("wqs2", [128, 2048], BF16, m["lntt0"]),
                      self.sb("wqs3", [128, 2048], BF16, m["lntt1"])]
        self.wo_sb = self.sb("wo_sb", [128, NIC * 2048], BF16, P)
        self.q_all = self.sb("q_all", [128, NIC * 1024], BF16, P + 65536)
        self.pq = [self.ps(f"pq{i}", [128, 2048]) for i in range(2)]
        wq_slot_free = [self.t_kv_done, self.t_kv_done]
        pq_free = [0, 0]
        wo3 = self.d_wo[:, :].rearrange("(c p) d -> c p d", p=128)
        self.wo_ld = 0
        self.w("tensor", "v", self.v_kvks_evac)
        self.w("vector", "v", self.v_kvks_evac)
        for icp in range(NIC // 2):
            psl = icp % 2
            self.w("gpsimd", "t", wq_slot_free[psl])
            lds = []
            for j in range(2):
                ic = icp * 2 + j
                lds.append(self.dma(
                    "gpsimd", f"wq{psl * 2 + j}",
                    self.wq_sb[psl * 2 + j][:, :].rearrange("p (c i) -> p c i", c=NDC),
                    self.d_wq[ic, :, :, :]))
                self.wo_ld = self.dma("gpsimd", "wo",
                                      self.wo_sb[:, ic * 2048:(ic + 1) * 2048],
                                      wo3[ic, :, :])
            self.w("tensor", "v", pq_free[psl])
            for j in range(2):
                self.w("tensor", f"wq{psl * 2 + j}", lds[j])
                for dc in range(NDC):
                    for th in range(2):
                        self.op("tensor", "matmul",
                                self.pq[psl][:, j * 1024 + th * 512:j * 1024 + (th + 1) * 512],
                                self.wq_sb[psl * 2 + j][:, dc * 128:(dc + 1) * 128],
                                self.ln_fm[:, dc * 1024 + th * 512:dc * 1024 + (th + 1) * 512],
                                start=(dc == 0), stop=(dc == NDC - 1),
                                skip_group_check=True)
            wq_slot_free[psl] = self.tmark()
            self.w("vector", "t", wq_slot_free[psl])
            pq_free[psl] = self.op("vector", "tensor_scalar_max",
                                   self.q_all[:, icp * 2048:(icp + 1) * 2048],
                                   self.pq[psl][:, :], 0.0, inc="v")
        self.v_q_done = self.cnt["v"]
        self.b1a_t_done = self.cnt["t"]

    def phase_b1b(self):
        self.collective_finish()
        self.free("pq1", "pq0")
        m = self.m
        self.po = [self.ps(f"po{i}", [128, 1024]) for i in range(2)]
        self.pnb = [self.ps(f"pnb{i}", [128, 1024]) for i in range(2)]
        self.rn_sb = [self.sb("rn0", [128, 1024], BF16, m["out1"]),
                      self.sb("rn1", [128, 1024], BF16, m["out1"] + 2048)]
        self.lnx = [self.sb("lnx0", [128, 1024], F32, m["lntt0"]),
                    self.sb("lnx1", [128, 1024], F32, m["lntt1"])]
        po_free = [self.v_q_done, self.v_q_done]
        pnb_free = [self.v_q_done, self.v_q_done]
        rn_free = [0, 0]
        self.w("tensor", "v", self.v_kvblk)
        self.w("scalar", "s", self.a0_s_end)
        for ic in range(NIC):
            sl = ic % 2
            self.w("tensor", "v", po_free[sl])
            for th in range(2):
                self.op("tensor", "matmul",
                        self.po[sl][:, th * 512:(th + 1) * 512],
                        self.kvblk[:, ic * 128:(ic + 1) * 128],
                        self.q_all[:, ic * 1024 + th * 512:ic * 1024 + (th + 1) * 512],
                        start=True, stop=True)
            self.w("tensor", "v", pnb_free[sl])
            for th in range(2):
                self.op("tensor", "matmul",
                        self.pnb[sl][:, th * 512:(th + 1) * 512],
                        self.ksrep[:, ic * 128:(ic + 1) * 128],
                        self.q_all[:, ic * 1024 + th * 512:ic * 1024 + (th + 1) * 512],
                        start=True, stop=True)
            t_nb = self.tmark()
            t_qkv = t_nb
            self.w("vector", "t", t_nb)
            self.w("vector", "s", rn_free[sl])
            v_rn = self.op("vector", "tensor_scalar_max", self.rn_sb[sl][:, :],
                           self.pnb[sl][:, :], 1e-6, inc="v")
            pnb_free[sl] = v_rn
            self.w("scalar", "v", v_rn)
            s_ln = self.op("scalar", "activation", self.lnx[sl][:, :],
                           self.rn_sb[sl][:, :], AF.Ln, inc="s")
            self.w("scalar", "s", s_ln)
            s_rcp = self.op("scalar", "activation", self.rn_sb[sl][:, :],
                            self.lnx[sl][:, :], AF.Exp, scale=-1.0, inc="s")
            rn_free[sl] = s_rcp
            self.w("vector", "s", s_rcp)
            self.w("vector", "t", t_qkv)
            po_free[sl] = self.op("vector", "tensor_tensor",
                                  self.q_all[:, ic * 1024:(ic + 1) * 1024],
                                  self.po[sl][:, :], self.rn_sb[sl][:, :],
                                  ALU.mult, inc="v")
        self.v_qkvn_done = self.cnt["v"]
        self.b1b_s_done = self.cnt["s"]

    def phase_b2c(self):
        self.free("pnb1", "pnb0", "po1", "po0")
        self.pat = self.ps("pat", [128, 2048])
        self.pTc = self.ps("pTc", [128, 2048], F32)
        lnfm3 = self.ln_fm[:, :].rearrange("p (c u) -> p c u", c=NDC)
        pat_free_v = self.v_qkvn_done
        self.v_pTc = [self.v_qkvn_done, self.v_qkvn_done]
        self.tp_batch = 0
        x_free = [self.a0_x_done[-2], self.a0_x_done[-1]]
        x1w_prev = [0, 0]
        self.s_ln3 = []
        self.b2_t_done = 0
        prev_tp = None
        for tt in range(self.B2C_TTS):
            slot = tt % 2
            vv, ss = x_free[slot]
            self.w("sync", "v", vv)
            self.w("sync", "s", ss)
            vx = self.dma("sync", f"x0{slot}", self.x_sb[slot][:, :],
                          self.d_x[tt * 128:(tt + 1) * 128, :])
            self.w("tensor", "v", pat_free_v)
            if tt == 0:
                self.w("tensor", "wo", self.wo_ld)
                self.w("tensor", "gb", 48)
            for dg in range(4):
                self.op("tensor", "matmul",
                        self.pat[:, dg * 512:(dg + 1) * 512],
                        self.ones_row,
                        self.bo_sb[0:1, dg * 512:(dg + 1) * 512],
                        start=True, stop=False, skip_group_check=True)
            for ic in range(NIC):
                for dg in range(4):
                    self.op("tensor", "matmul",
                            self.pat[:, dg * 512:(dg + 1) * 512],
                            self.q_all[:, ic * 1024 + tt * 128:ic * 1024 + (tt + 1) * 128],
                            self.wo_sb[:, ic * 2048 + dg * 512:ic * 2048 + (dg + 1) * 512],
                            start=False, stop=(ic == NIC - 1), skip_group_check=True)
            t_attn = self.tmark()
            self.b2_t_done = t_attn
            if prev_tp is not None:
                pslot, ps_ap3, ptt = prev_tp
                self.w("tensor", "s", ps_ap3)
                self.transpose_batches(self.pTc, self.lntt[pslot], ptt,
                                       self.v_pTc, lnfm3)
            self.w("vector", "t", t_attn)
            v_cp = self.op("vector", "tensor_copy", self.x1_sb[slot][:, :],
                           self.pat[:, :], inc="v")
            pat_free_v = v_cp
            self.vsync(v_cp)
            self.w("scalar", "v", v_cp)
            v_rc2, s_sq2 = self.ln_stats(self.x1_sb[slot][:, :], slot)
            r2, nm2, s_rs2 = self.ln_rs(slot, v_rc2)
            self.w("vector", f"x0{slot}", vx)
            self.w("vector", "s", s_rs2)
            if tt >= 2:
                self.w("vector", f"xw{slot}", x1w_prev[slot])
            va = self.op("vector", "tensor_scalar_add", self.x_sb[slot][:, :],
                         self.x_sb[slot][:, :], nm2, inc="v")
            self.vsync(va)
            v_x1 = self.op("vector", "scalar_tensor_tensor", self.x1_sb[slot][:, :],
                           self.x1_sb[slot][:, :], r2, self.x_sb[slot][:, :],
                           ALU.mult, ALU.add, inc="v")
            self.w("sync", "v", v_x1)
            x1w_prev[slot] = self.dma("sync", f"xw{slot}",
                                      self.d_x1[tt * 128:(tt + 1) * 128, :],
                                      self.x1_sb[slot][:, :])
            self.w("scalar", "v", v_x1)
            self.vsync(v_x1)
            v_rc3, _ = self.ln_stats(self.x1_sb[slot][:, :], slot)
            r3, nm3, _ = self.ln_rs(slot, v_rc3)
            s_ap3 = self.op("scalar", "activation", self.lntt[slot][:, :],
                            self.x1_sb[slot][:, :], AF.Identity, bias=nm3, scale=r3,
                            inc="s")
            self.s_ln3.append(s_ap3)
            x_free[slot] = (v_x1, s_ap3)
            prev_tp = (slot, s_ap3, tt)
        if prev_tp is not None:
            pslot, ps_ap3, ptt = prev_tp
            self.w("tensor", "s", ps_ap3)
            self.transpose_batches(self.pTc, self.lntt[pslot], ptt,
                                   self.v_pTc, lnfm3)
        self.c_v_done = self.cnt["v"]
        self.c_s_done = self.cnt["s"]
        self.x1w_all = (self.cnt["xw0"], self.cnt["xw1"])

    def phase_d1(self):
        self.free("pTc", "pat")
        m, P = self.m, self.P
        self.pgu = [self.ps(f"pgu{i}", [128, 2048]) for i in range(2)]
        self.wg_sb = [self.sb("wgs0", [128, 2048], BF16, m["kvblk"]),
                      self.sb("wgs1", [128, 2048], BF16, m["ksrep"])]
        self.wu_sb = [self.sb("wus0", [128, 2048], BF16, P),
                      self.sb("wus1", [128, 2048], BF16, P + 4096)]
        self.gate_f = [self.sb("gf0", [128, 1024], F32, m["x0"]),
                       self.sb("gf1", [128, 1024], F32, m["x0"] + 4096)]
        self.up_f = [self.sb("uf0", [128, 1024], F32, m["x1s"]),
                     self.sb("uf1", [128, 1024], F32, m["x1s"] + 4096)]
        self.hidden = self.sb("hidden", [128, NHC * 1024], BF16, P + 8192)
        t_wg_free = [self.b2_t_done, self.b2_t_done]
        t_wu_free = [self.b2_t_done, self.b2_t_done]
        pg_free = [0, 0]
        pu_free = [0, 0]
        gfuf_free = [0, 0]
        self.w("gpsimd", "v", self.c_v_done)
        self.w("gpsimd", "s", self.c_s_done)
        self.w("tensor", "v", self.c_v_done)
        self.w("tensor", "s", self.c_s_done)
        self.w("scalar", "c", 64)
        self.w("vector", "c", 64)
        for hc in range(NHC):
            sl = hc % 2
            self.w("gpsimd", "t", t_wg_free[sl])
            ldg = self.dma("gpsimd", f"wg{sl}",
                           self.wg_sb[sl][:, :].rearrange("p (c i) -> p c i", c=NDC),
                           self.d_wg[hc, :, :, :])
            self.w("gpsimd", "t", t_wu_free[sl])
            ldu = self.dma("gpsimd", f"wu{sl}",
                           self.wu_sb[sl][:, :].rearrange("p (c i) -> p c i", c=NDC),
                           self.d_wu[hc, :, :, :])
            self.w("vector", f"wg{sl}", ldg)
            v_bg = self.op("vector", "tensor_scalar_add", self.wg_sb[sl][:, :],
                           self.wg_sb[sl][:, :], -128.0, inc="v")
            self.w("vector", f"wu{sl}", ldu)
            v_bu = self.op("vector", "tensor_scalar_add", self.wu_sb[sl][:, :],
                           self.wu_sb[sl][:, :], -128.0, inc="v")
            self.w("tensor", "v", v_bu)
            self.w("tensor", "s", pg_free[sl])
            self.w("tensor", "v", pu_free[sl])
            for dc in range(NDC):
                for th in range(2):
                    self.op("tensor", "matmul",
                            self.pgu[sl][:, th * 512:(th + 1) * 512],
                            self.wg_sb[sl][:, dc * 128:(dc + 1) * 128],
                            self.ln_fm[:, dc * 1024 + th * 512:dc * 1024 + (th + 1) * 512],
                            start=(dc == 0), stop=(dc == NDC - 1),
                            skip_group_check=True)
            for dc in range(NDC):
                for th in range(2):
                    self.op("tensor", "matmul",
                            self.pgu[sl][:, 1024 + th * 512:1024 + (th + 1) * 512],
                            self.wu_sb[sl][:, dc * 128:(dc + 1) * 128],
                            self.ln_fm[:, dc * 1024 + th * 512:dc * 1024 + (th + 1) * 512],
                            start=(dc == 0), stop=(dc == NDC - 1),
                            skip_group_check=True)
            t_u = self.tmark()
            t_wg_free[sl] = t_u
            t_wu_free[sl] = t_u
            self.w("scalar", "t", t_u)
            self.w("scalar", "v", gfuf_free[sl])
            s_si = self.op("scalar", "activation", self.gate_f[sl][:, :],
                           self.pgu[sl][:, 0:1024], AF.Silu,
                           scale=self.sg_sb[:, hc:hc + 1], inc="s")
            pg_free[sl] = s_si
            self.w("vector", "t", t_u)
            vu = self.op("vector", "tensor_scalar_mul", self.up_f[sl][:, :],
                         self.pgu[sl][:, 1024:2048], self.su_sb[:, hc:hc + 1],
                         inc="v")
            pu_free[sl] = vu
            self.w("vector", "s", s_si)
            self.vsync(vu)
            v_h = self.op("vector", "tensor_tensor",
                          self.hidden[:, hc * 1024:(hc + 1) * 1024],
                          self.gate_f[sl][:, :], self.up_f[sl][:, :],
                          ALU.mult, inc="v")
            gfuf_free[sl] = v_h
        self.v_hid_done = self.cnt["v"]
        self.d1_t_done = self.cnt["t"]
        self.d1_s_done = self.cnt["s"]

    def phase_d2(self):
        self.free("pgu1", "pgu0")
        m = self.m
        self.pd = [self.ps(f"pd{i}", [128, 1024]) for i in range(2)]
        self.wd_sb = [self.sb("wds0", [128, NHC * 128], BF16, m["out0"]),
                      self.sb("wds1", [128, NHC * 128], BF16, m["x0"])]
        self.down_fm = self.sb("down_fm", [128, NDC * 1024], BF16, m["ln_fm"])
        t_wd_free = [self.d1_t_done, self.d1_t_done]
        pd_free = [self.v_hid_done, self.v_hid_done]
        self.w("tensor", "s", self.d1_s_done)
        self.w("gpsimd", "v", self.v_hid_done)
        self.w("gpsimd", "s", self.d1_s_done)
        self.w("vector", "c", 64)
        for dc in range(NDC):
            sl = dc % 2
            self.w("gpsimd", "t", t_wd_free[sl])
            ld = self.dma("gpsimd", f"wd{sl}",
                          self.wd_sb[sl][:, :].rearrange("p (c i) -> p c i", c=NHC),
                          self.d_wd[dc, :, :, :])
            self.w("vector", f"wd{sl}", ld)
            v_b = self.op("vector", "tensor_scalar_add", self.wd_sb[sl][:, :],
                          self.wd_sb[sl][:, :], -128.0, inc="v")
            self.w("tensor", "v", v_b)
            self.w("tensor", "v", pd_free[sl])
            for hc in range(NHC):
                for th in range(2):
                    self.op("tensor", "matmul",
                            self.pd[sl][:, th * 512:(th + 1) * 512],
                            self.wd_sb[sl][:, hc * 128:(hc + 1) * 128],
                            self.hidden[:, hc * 1024 + th * 512:hc * 1024 + (th + 1) * 512],
                            start=(hc == 0), stop=(hc == NHC - 1),
                            skip_group_check=True)
            t_d = self.tmark()
            t_wd_free[sl] = t_d
            self.w("vector", "t", t_d)
            pd_free[sl] = self.op("vector", "tensor_scalar_mul",
                                  self.down_fm[:, dc * 1024:(dc + 1) * 1024],
                                  self.pd[sl][:, :], self.sd_sb[:, dc:dc + 1],
                                  inc="v")
        self.v_dfm_done = self.cnt["v"]
        self.d2_t_done = self.cnt["t"]

    def phase_d3(self):
        self.free("pd1", "pd0")
        m = self.m
        self.out_sb = [self.sb("outs0", [128, D], F32, m["out0"]),
                       self.sb("outs1", [128, D], F32, m["out1"])]
        self.pt2 = [self.ps(f"pt2{i}", [128, 2048], F32) for i in range(2)]
        pt2_free = [(self.v_dfm_done, self.d1_s_done),
                    (self.v_dfm_done, self.d1_s_done)]
        out_free = [0, 0]
        x1_slot_free = [0, 0]
        self.w("sync", "s", self.s_ln3[-1])
        self.w("sync", "xw0", self.x1w_all[0])
        self.w("sync", "xw1", self.x1w_all[1])
        y_lasts = [0, 0]
        for tt in range(NTT):
            slot = tt % 2
            if tt >= 2:
                self.w("sync", "v", x1_slot_free[slot])
            vx = self.dma("sync", f"xr{slot}", self.x1_sb[slot][:, :],
                          self.d_x1[tt * 128:(tt + 1) * 128, :])
            self.w("tensor", "v", pt2_free[slot][0])
            self.w("tensor", "s", pt2_free[slot][1])
            for dc in range(NDC):
                self.op("tensor", "matmul",
                        self.pt2[slot][:, dc * 128:(dc + 1) * 128],
                        self.down_fm[:, dc * 1024 + tt * 128:dc * 1024 + (tt + 1) * 128],
                        self.ident, start=(dc % 4 == 0), stop=True,
                        skip_group_check=True)
            t_t4 = self.tmark()
            self.w("vector", "t", t_t4)
            self.w("vector", f"y{slot}", out_free[slot])
            v_cp4 = self.op("vector", "tensor_copy", self.out_sb[slot][:, :],
                            self.pt2[slot][:, :], inc="v")
            pt2_free[slot] = (v_cp4, 0)
            self.vsync(v_cp4)
            self.w("scalar", "v", v_cp4)
            v_ve4, s_sq4 = self.ln_stats(self.out_sb[slot][:, :], slot)
            r4, nm4, _ = self.ln_rs_newton(slot, v_ve4)
            self.w("vector", f"xr{slot}", vx)
            va = self.op("vector", "tensor_scalar_add", self.x1_sb[slot][:, :],
                         self.x1_sb[slot][:, :], nm4, inc="v")
            self.vsync(va)
            v_out = self.op("vector", "scalar_tensor_tensor", self.out_sb[slot][:, :],
                            self.out_sb[slot][:, :], r4, self.x1_sb[slot][:, :],
                            ALU.mult, ALU.add, inc="v")
            x1_slot_free[slot] = v_out
            self.w("sync", "v", v_out)
            y_lasts[slot] = self.dma("sync", f"y{slot}",
                                     self.d_y[tt * 128:(tt + 1) * 128, :],
                                     self.out_sb[slot][:, :])
            out_free[slot] = y_lasts[slot]
        self.w("sync", "y0", y_lasts[0])
        self.w("sync", "y1", y_lasts[1])

    def finalize(self):
        import functools
        blk_guard = self.nc.Block()
        blk = blk_guard.__enter__()
        for eng in ENGINES:
            getattr(blk, eng)(functools.partial(_run, lst=self.ops[eng]))
        blk_guard.__exit__(None, None, None)


_BUILT = None


def _get_nc():
    global _BUILT
    if _BUILT is None:
        _BUILT = MK().nc
    return _BUILT


def kernel(L0_attn_hidden_states, L0_attn_Wq, L0_attn_Wk, L0_attn_Wv, L0_attn_Wo,
           L0_attn_bo, L0_ffn_wq_gate, L0_ffn_s_gate, L0_ffn_wq_up, L0_ffn_s_up,
           L0_ffn_wq_down, L0_ffn_s_down):
    x2d = np.ascontiguousarray(np.asarray(L0_attn_hidden_states,
                                          dtype=np.float32).reshape(S_FULL, D))
    wq = np.ascontiguousarray(np.asarray(L0_attn_Wq, np.float32)
                              .reshape(NDC, 128, NIC, 128).transpose(2, 1, 0, 3))
    wk = np.ascontiguousarray(np.asarray(L0_attn_Wk, np.float32)
                              .reshape(NDC, 128, NIG, 512).transpose(2, 1, 0, 3))
    wv = np.ascontiguousarray(np.asarray(L0_attn_Wv, np.float32)
                              .reshape(NDC, 128, NIG, 512).transpose(2, 1, 0, 3))
    wo = np.ascontiguousarray(np.asarray(L0_attn_Wo, np.float32))
    bo = np.ascontiguousarray(np.asarray(L0_attn_bo, np.float32).reshape(1, D))
    wg = np.ascontiguousarray(np.asarray(L0_ffn_wq_gate, np.int32)
                              .reshape(NHC, 128, NDC, 128).transpose(0, 3, 2, 1))
    wu = np.ascontiguousarray(np.asarray(L0_ffn_wq_up, np.int32)
                              .reshape(NHC, 128, NDC, 128).transpose(0, 3, 2, 1))
    wd = np.ascontiguousarray(np.asarray(L0_ffn_wq_down, np.int32)
                              .reshape(NDC, 128, NHC, 128).transpose(0, 3, 2, 1))
    sg = np.ascontiguousarray(np.asarray(L0_ffn_s_gate, np.float32).reshape(NHC, 128).T)
    su = np.ascontiguousarray(np.asarray(L0_ffn_s_up, np.float32).reshape(NHC, 128).T)
    sd = np.ascontiguousarray(np.asarray(L0_ffn_s_down, np.float32).reshape(NDC, 128).T)
    cst = np.zeros((128, 512), dtype=ml_dtypes.bfloat16)
    cst[:, 0:128] = np.eye(128, dtype=ml_dtypes.bfloat16)
    cst[0:64, 128:192] = 1
    cst[64:128, 192:256] = 1
    cst[:, 256] = 1
    cst[:, 258:386] = 1

    nc = _get_nc()
    in_maps = []
    for c in range(NCORES):
        in_maps.append({
            "x": np.ascontiguousarray(x2d[c * T:(c + 1) * T]),
            "wq": wq, "wk": wk, "wv": wv, "wo": wo, "bo": bo,
            "wg": wg, "wu": wu, "wd": wd, "sg": sg, "su": su, "sd": sd,
            "cst": cst,
        })
    res = run_bass_kernel_spmd(nc, in_maps, core_ids=list(range(NCORES)))
    out = np.concatenate([res.results[c]["y"] for c in range(NCORES)], axis=0)
    return out.reshape(1, S_FULL, D).astype(np.float32)


# version 14 (final)
# speedup vs baseline: 1.0468x; 1.0468x over previous
import numpy as np
import ml_dtypes
import concourse.bass as bass
import concourse.mybir as mybir
from concourse.bass_utils import run_bass_kernel_spmd

F32 = mybir.dt.float32
BF16 = mybir.dt.bfloat16
I32 = mybir.dt.int32
AF = mybir.ActivationFunctionType
ALU = mybir.AluOpType

NCORES = 8
S_FULL, D, HID = 8192, 2048, 5632
T = S_FULL // NCORES
NTT = T // 128
NDC = D // 128
NIC = D // 128
NIG = 4
NHC = HID // 128
EPS = 1e-5
ENGINES = ("sync", "gpsimd", "tensor", "vector", "scalar")


def _run(e, lst):
    for f in lst:
        f(e)


class MK:
    PHASES = ("a0", "ak", "av", "cc", "b1a", "b1b", "b2c", "d1", "d2", "d3")
    B2C_TTS = NTT
    B2C_TAIL = True

    def __init__(self, phases=None, debug_out=None):
        self.phases = self.PHASES if phases is None else phases
        self.debug_out = debug_out
        nc = bass.Bass("TRN2", target_bir_lowering=False, debug=False)
        self.nc = nc
        self.ops = {e: [] for e in ENGINES}
        self.sems = {}
        self.cnt = {}
        self.guards = []
        self.live = {}
        for s in ("c", "x00", "x01", "xw0", "xw1", "xr0", "xr1", "y0", "y1",
                  "wa0", "wa1", "wq0", "wq1", "wq2", "wq3", "wo", "wg0", "wg1",
                  "wu0", "wu1", "wd0", "wd1", "gb",
                  "cc", "t", "v", "s"):
            g = nc.semaphore(s)
            self.sems[s] = g.__enter__()
            self.cnt[s] = 0
            self.guards.append(g)
        self.declare_dram()
        self.alloc_global()
        self.walk()
        self.finalize()

    def emit(self, eng, f):
        self.ops[eng].append(f)

    def w(self, eng, sname, val):
        if val is None or val <= 0:
            return
        self.emit(eng, lambda e, s=self.sems[sname], v=val: e.wait_ge(s, v))

    def dma(self, eng, sname, out_ap, in_ap):
        self.cnt[sname] += 16
        self.emit(eng, lambda e, o=out_ap, i=in_ap, s=self.sems[sname]:
                  e.dma_start(out=o, in_=i).then_inc(s, 16))
        return self.cnt[sname]

    def op(self, eng, fname, *args, inc=None, **kw):
        semv = None
        s = None
        if inc is not None:
            self.cnt[inc] += 1
            semv = self.cnt[inc]
            s = self.sems[inc]
        def f(e, fn=fname, a=args, k=kw, s=s):
            ins = getattr(e, fn)(*a, **k)
            if s is not None:
                ins.then_inc(s, 1)
        self.emit(eng, f)
        return semv

    def tmark(self):
        self.cnt["t"] += 1
        v = self.cnt["t"]
        self.emit("tensor", lambda e, s=self.sems["t"]: e.drain().then_inc(s, 1))
        return v

    def vsync(self, v):
        self.w("vector", "v", v)

    def sb(self, name, shape, dt, off):
        return self.nc.alloc_sbuf_tensor_at(name, shape, dt, offset=off)

    def ps(self, name, shape, dt=F32):
        g = self.nc.psum_tensor(name, shape, dt)
        t = g.__enter__()
        self.guards.append(g)
        self.live[name] = g
        return t

    def free(self, *names):
        for n in names:
            self.live.pop(n).__exit__(None, None, None)

    def declare_dram(self):
        nc = self.nc
        dt = nc.dram_tensor
        self.d_x = dt("x", [T, D], F32, kind="ExternalInput")
        self.d_wq = dt("wq", [NIC, 128, NDC, 128], F32, kind="ExternalInput")
        self.d_wk = dt("wk", [NIG, 128, NDC, 512], F32, kind="ExternalInput")
        self.d_wv = dt("wv", [NIG, 128, NDC, 512], F32, kind="ExternalInput")
        self.d_wo = dt("wo", [D, D], F32, kind="ExternalInput")
        self.d_bo = dt("bo", [1, D], F32, kind="ExternalInput")
        self.d_wg = dt("wg", [NHC, 128, NDC, 128], I32, kind="ExternalInput")
        self.d_wu = dt("wu", [NHC, 128, NDC, 128], I32, kind="ExternalInput")
        self.d_wd = dt("wd", [NDC, 128, NHC, 128], I32, kind="ExternalInput")
        self.d_sg = dt("sg", [128, NHC], F32, kind="ExternalInput")
        self.d_su = dt("su", [128, NHC], F32, kind="ExternalInput")
        self.d_sd = dt("sd", [128, NDC], F32, kind="ExternalInput")
        self.d_cst = dt("cst", [128, 512], BF16, kind="ExternalInput")
        self.d_y = dt("y", [T, D], F32, kind="ExternalOutput")
        self.d_x1 = dt("x1d", [T, D], F32)
        self.d_cci = dt("cc_in", [128, 1040], F32)
        self.d_cco = dt("cc_out", [128, 1040], F32)

    def alloc_global(self):
        B = 17408
        m = {}
        off = B
        for name, nbytes in [("cst_sb", 1024), ("tiny", 128), ("sg_sb", 192),
                             ("su_sb", 192), ("sd_sb", 64), ("bo_sb", 4096),
                             ("scr", 4096), ("scr2", 4096), ("lntt0", 4096), ("lntt1", 4096),
                             ("x1_0", 8192), ("x1_1", 8192), ("kvblk", 4096),
                             ("ksrep", 4096), ("ln_fm", 32768), ("x0", 8192),
                             ("x1s", 8192), ("out0", 8192), ("out1", 8192)]:
            m[name] = off
            off += nbytes
        self.P = off
        self.m = m
        assert self.P + 8192 + 90112 <= 229376, self.P
        self.ln_fm = self.sb("ln_fm", [128, NDC * 1024], BF16, m["ln_fm"])
        self.x_sb = [self.sb("xs0", [128, D], F32, m["x0"]),
                     self.sb("xs1", [128, D], F32, m["x1s"])]
        self.x1_sb = [self.sb("x1_0", [128, D], F32, m["x1_0"]),
                      self.sb("x1_1", [128, D], F32, m["x1_1"])]
        self.lntt = [self.sb("lntt0", [128, D], BF16, m["lntt0"]),
                     self.sb("lntt1", [128, D], BF16, m["lntt1"])]
        self.scrV = self.sb("scrV", [128, D], BF16, m["scr"])
        self.scrS = self.sb("scrS", [128, D], BF16, m["scr2"])
        self.cst = self.sb("cst_sb", [128, 512], BF16, m["cst_sb"])
        self.tiny = self.sb("tiny", [128, 20], F32, m["tiny"])
        self.kvblk = self.sb("kvblk", [128, 2048], BF16, m["kvblk"])
        self.ksrep = self.sb("ksrep", [128, 2048], BF16, m["ksrep"])
        self.bo_sb = self.sb("bo_sb", [1, D], BF16, m["bo_sb"])
        self.sg_sb = self.sb("sg_sb", [128, NHC], F32, m["sg_sb"])
        self.su_sb = self.sb("su_sb", [128, NHC], F32, m["su_sb"])
        self.sd_sb = self.sb("sd_sb", [128, NDC], F32, m["sd_sb"])
        self.ident = self.cst[:, 0:128]
        self.blockmask = self.cst[:, 128:256]
        self.ones_col = self.cst[:, 256:257]
        self.ones_row = self.cst[0:1, 258:386]

    def tv(self, i, slot):
        c = i * 2 + slot
        return self.tiny[:, c:c + 1]

    def ln_stats(self, src_ap, slot, psum=False):
        sumv, sumsq = self.tv(0, slot), self.tv(1, slot)
        mneg, ve, m2, rc = (self.tv(2, slot), self.tv(3, slot),
                            self.tv(4, slot), self.tv(7, slot))
        n_inv = 1.0 / D
        v0 = self.op("vector", "tensor_scalar", self.scrV[:, :], src_ap, 1.0, 0.0,
                     ALU.mult, ALU.add, accum_out=sumv, inc="v")
        if psum:
            self.w("scalar", "v", v0)
        s_sq = self.op("scalar", "activation", self.scrS[:, :], src_ap, AF.Square,
                       accum_out=sumsq, inc="s")
        self.vsync(v0)
        v1 = self.op("vector", "tensor_scalar", mneg, sumv, -n_inv, 0.0,
                     ALU.mult, ALU.add, inc="v")
        self.vsync(v1)
        v2 = self.op("vector", "tensor_mul", m2, mneg, mneg, inc="v")
        self.w("vector", "s", s_sq)
        v3 = self.op("vector", "tensor_scalar", ve, sumsq, n_inv, EPS,
                     ALU.mult, ALU.add, inc="v")
        self.vsync(v3)
        self.vsync(v2)
        v4 = self.op("vector", "tensor_sub", ve, ve, m2, inc="v")
        self.vsync(v4)
        return v4, s_sq

    def ln_rs_newton(self, slot, v_ve):
        ve, y, t1, t2 = (self.tv(3, slot), self.tv(5, slot),
                         self.tv(4, slot), self.tv(7, slot))
        nm = self.tv(6, slot)
        self.vsync(v_ve)
        vv = self.op("vector", "reciprocal", y, ve, inc="v")
        self.vsync(vv)
        vv = self.op("vector", "tensor_scalar_min", y, y, 1.0, inc="v")
        for _ in range(10):
            self.vsync(vv)
            vv = self.op("vector", "tensor_mul", t1, y, y, inc="v")
            self.vsync(vv)
            vv = self.op("vector", "tensor_mul", t2, t1, ve, inc="v")
            self.vsync(vv)
            vv = self.op("vector", "tensor_scalar", t1, t2, -0.5, 1.5,
                         ALU.mult, ALU.add, inc="v")
            self.vsync(vv)
            vv = self.op("vector", "tensor_mul", y, y, t1, inc="v")
        self.vsync(vv)
        v_cnt = self.op("vector", "tensor_mul", nm, self.tv(2, slot), y, inc="v")
        self.vsync(v_cnt)
        return y, nm, v_cnt

    def ln_rs(self, slot, v_ve):
        r, nm, lnv = self.tv(5, slot), self.tv(6, slot), self.tv(7, slot)
        self.w("scalar", "v", v_ve)
        s0 = self.op("scalar", "activation", lnv, self.tv(3, slot), AF.Ln, inc="s")
        self.w("scalar", "s", s0)
        s1 = self.op("scalar", "activation", r, lnv, AF.Exp, scale=-0.5, inc="s")
        self.w("scalar", "s", s1)
        s_cnt = self.op("scalar", "mul", nm, self.tv(2, slot), r, inc="s")
        self.w("scalar", "s", s_cnt)
        return r, nm, s_cnt

    def transpose_batches(self, pTile, src_tile, tt, free_half, lnfm3):
        for half_i in range(2):
            hp = self.tp_batch % 2
            self.w("tensor", "v", free_half[hp])
            for j in range(8):
                dc = half_i * 8 + j
                self.op("tensor", "matmul",
                        pTile[:, hp * 1024 + j * 128:hp * 1024 + (j + 1) * 128],
                        src_tile[:, dc * 128:(dc + 1) * 128],
                        self.ident, start=(j % 4 == 0), stop=True,
                        skip_group_check=True)
            tb = self.tmark()
            self.w("vector", "t", tb)
            pv = pTile[:, hp * 1024:(hp + 1) * 1024].rearrange(
                "p (c u) -> p c u", c=8)
            free_half[hp] = self.op(
                "vector", "tensor_copy",
                lnfm3[:, half_i * 8:half_i * 8 + 8, tt * 128:(tt + 1) * 128],
                pv, inc="v")
            self.tp_batch += 1

    def walk(self):
        self.consts()
        table = [("a0", self.phase_a0), ("ak", self.phase_ak),
                 ("av", self.phase_av), ("cc", self.collective),
                 ("b1a", self.phase_b1a), ("b1b", self.phase_b1b),
                 ("b2c", self.phase_b2c), ("d1", self.phase_d1),
                 ("d2", self.phase_d2), ("d3", self.phase_d3)]
        for name, fn in table:
            if name in self.phases:
                fn()
        if self.debug_out is not None:
            self.debug_out(self)

    def consts(self):
        self.dma("sync", "c", self.cst[:, :], self.d_cst[:, :])
        self.dma("sync", "c", self.sg_sb[:, :], self.d_sg[:, :])
        self.dma("sync", "c", self.su_sb[:, :], self.d_su[:, :])
        self.dma("sync", "c", self.sd_sb[:, :], self.d_sd[:, :])
        self.dma("gpsimd", "gb", self.bo_sb[:, :], self.d_bo[:, :])

    def phase_a0(self):
        self.pT = [self.ps(f"pT{i}", [128, 2048], F32) for i in range(2)]
        self.a0_x_done = []
        self.v_pT = [0, 0]
        self.s_apply = []
        lnfm3 = self.ln_fm[:, :].rearrange("p (c u) -> p c u", c=NDC)
        self.w("tensor", "c", 64)
        for tt in range(NTT):
            slot = tt % 2
            if tt >= 2:
                vv, ss = self.a0_x_done[tt - 2]
                self.w("sync", "v", vv)
                self.w("sync", "s", ss)
            vx = self.dma("sync", f"x0{slot}", self.x_sb[slot][:, :],
                          self.d_x[tt * 128:(tt + 1) * 128, :])
            self.w("vector", f"x0{slot}", vx)
            self.w("scalar", f"x0{slot}", vx)
            v_rc, _ = self.ln_stats(self.x_sb[slot][:, :], slot)
            v_stats = self.cnt["v"]
            r, nm, _ = self.ln_rs(slot, v_rc)
            s_ap = self.op("scalar", "activation", self.lntt[slot][:, :],
                           self.x_sb[slot][:, :], AF.Identity, bias=nm, scale=r,
                           inc="s")
            self.s_apply.append(s_ap)
            self.a0_x_done.append((v_stats, s_ap))
            self.w("tensor", "s", s_ap)
            self.w("tensor", "v", self.v_pT[slot])
            for dc in range(NDC):
                self.op("tensor", "matmul",
                        self.pT[slot][:, dc * 128:(dc + 1) * 128],
                        self.lntt[slot][:, dc * 128:(dc + 1) * 128],
                        self.ident, start=(dc % 4 == 0), stop=True,
                        skip_group_check=True)
            tb = self.tmark()
            self.w("vector", "t", tb)
            pv = self.pT[slot][:, :].rearrange("p (c u) -> p c u", c=NDC)
            self.v_pT[slot] = self.op(
                "vector", "tensor_copy",
                lnfm3[:, :, tt * 128:(tt + 1) * 128], pv, inc="v")
        self.a0_v_end = self.cnt["v"]
        self.a0_s_end = self.cnt["s"]

    def phase_ak(self):
        self.free("pT1", "pT0")
        P = self.P
        self.wkv_sb = [self.sb("wkv0", [128, NDC * 512], BF16, P),
                       self.sb("wkv1", [128, NDC * 512], BF16, P + 16384)]
        self.k_tm = self.sb("k_tm", [128, NTT * 2048], BF16, P + 32768)
        self.pa = self.ps("pa", [128, 2048])
        self.t_wslot = [0, 0]
        self.v_pa = [0, 0]
        pidx = 0
        self.w("tensor", "v", self.a0_v_end)
        for ig in range(NIG):
            slot = ig % 2
            self.w("gpsimd", "t", self.t_wslot[slot])
            ld = self.dma("gpsimd", f"wa{slot}",
                          self.wkv_sb[slot][:, :].rearrange("p (c i) -> p c i", c=NDC),
                          self.d_wk[ig, :, :, :])
            self.w("tensor", f"wa{slot}", ld)
            for tp in range(NTT // 2):
                ph = pidx % 2
                self.w("tensor", "v", self.v_pa[ph])
                for j in range(2):
                    tt = tp * 2 + j
                    col = ph * 1024 + j * 512
                    for dc in range(NDC):
                        self.op("tensor", "matmul", self.pa[:, col:col + 512],
                                self.ln_fm[:, dc * 1024 + tt * 128:dc * 1024 + (tt + 1) * 128],
                                self.wkv_sb[slot][:, dc * 512:(dc + 1) * 512],
                                start=(dc == 0), stop=(dc == NDC - 1))
                tm = self.tmark()
                self.w("vector", "t", tm)
                tt0 = tp * 2
                ktm3 = self.k_tm[:, :].rearrange("p (a u) -> p a u", a=NTT)
                pa3 = self.pa[:, ph * 1024:(ph + 1) * 1024].rearrange(
                    "p (a u) -> p a u", a=2)
                self.v_pa[ph] = self.op(
                    "vector", "tensor_scalar_max",
                    ktm3[:, tt0:tt0 + 2, ig * 512:(ig + 1) * 512],
                    pa3, 0.0, inc="v")
                pidx += 1
            self.t_wslot[slot] = self.cnt["t"]
        self.ak_idx = pidx

    def phase_av(self):
        P = self.P
        self.v_tt = [self.sb("vtt0", [128, 1024], BF16, P + 65536),
                     self.sb("vtt1", [128, 1024], BF16, P + 67584)]
        self.kvks = self.sb("kvks", [128, 1040], F32, P + 69632)
        self.kvp = self.ps("kvp", [128, 1024])
        self.ksp = self.ps("ksp", [128, 1024])
        pidx = self.ak_idx
        vslot_free = [0, 0]
        self.w("tensor", "c", 64)
        kv_started = set()
        ks_started = set()
        for ig in range(NIG):
            slot = ig % 2
            self.w("gpsimd", "t", self.t_wslot[slot])
            ld = self.dma("gpsimd", f"wa{slot}",
                          self.wkv_sb[slot][:, :].rearrange("p (c i) -> p c i", c=NDC),
                          self.d_wv[ig, :, :, :])
            self.w("tensor", f"wa{slot}", ld)
            for tp in range(NTT // 2):
                ph = pidx % 2
                vs = pidx % 2
                self.w("tensor", "v", self.v_pa[ph])
                for j in range(2):
                    tt = tp * 2 + j
                    col = ph * 1024 + j * 512
                    for dc in range(NDC):
                        self.op("tensor", "matmul", self.pa[:, col:col + 512],
                                self.ln_fm[:, dc * 1024 + tt * 128:dc * 1024 + (tt + 1) * 128],
                                self.wkv_sb[slot][:, dc * 512:(dc + 1) * 512],
                                start=(dc == 0), stop=(dc == NDC - 1))
                tm = self.tmark()
                self.w("vector", "t", tm)
                self.w("vector", "t", vslot_free[vs])
                v_ev = self.op("vector", "tensor_copy", self.v_tt[vs][:, :],
                               self.pa[:, ph * 1024:(ph + 1) * 1024], inc="v")
                self.v_pa[ph] = v_ev
                self.w("tensor", "v", v_ev)
                for j in range(2):
                    tt = tp * 2 + j
                    for hl in range(8):
                        h = ig * 8 + hl
                        po = 64 * (h % 2)
                        kslice = self.k_tm[:, tt * 2048 + h * 64:tt * 2048 + (h + 1) * 64]
                        kvkey = ((h // 2) // 8, h % 2)
                        self.op("tensor", "matmul",
                                self.kvp[po:po + 64, (h // 2) * 64:(h // 2 + 1) * 64],
                                kslice, self.v_tt[vs][:, j * 512 + hl * 64:j * 512 + (hl + 1) * 64],
                                start=(kvkey not in kv_started),
                                stop=(tt == NTT - 1), skip_group_check=True)
                        kv_started.add(kvkey)
                        kskey = ((h // 2) // 8, h % 2)
                        self.op("tensor", "matmul",
                                self.ksp[po:po + 64, (h // 2) * 64:(h // 2 + 1) * 64],
                                kslice, self.cst[:, 258:322],
                                start=(kskey not in ks_started),
                                stop=(tt == NTT - 1), skip_group_check=True)
                        ks_started.add(kskey)
                vslot_free[vs] = self.tmark()
                pidx += 1
            self.t_wslot[slot] = self.cnt["t"]
        self.t_kv_done = self.cnt["t"]

    def collective(self):
        self.w("vector", "t", self.t_kv_done)
        self.op("vector", "tensor_copy", self.kvks[:, 0:1024], self.kvp[:, :])
        v_kv = self.op("vector", "tensor_copy",
                       self.kvks[:, 1024:1040].rearrange("p (c u) -> p c u", c=16),
                       self.ksp[:, :].rearrange("p (c u) -> p c u", c=16)[:, :, 0:1],
                       inc="v")
        self.v_kvks_evac = v_kv
        self.w("gpsimd", "v", v_kv)
        g1 = self.dma("gpsimd", "gb", self.d_cci[:, :], self.kvks[:, :])
        self.w("gpsimd", "gb", g1)
        self.cnt["cc"] += 1
        self.emit("gpsimd", lambda e, ci=self.d_cci, co=self.d_cco,
                  s=self.sems["cc"]:
                  e.collective_compute(
                      "AllReduce", ALU.add,
                      replica_groups=[list(range(NCORES))],
                      ins=[ci.ap().opt()], outs=[co.ap().opt()]).then_inc(s, 1))

    def collective_finish(self):
        kvks2 = self.sb("kvks2", [128, 1040], F32, self.m["x1_0"])
        self.kvks2 = kvks2
        self.w("gpsimd", "cc", self.cnt["cc"])
        g2 = self.dma("gpsimd", "gb", kvks2[:, :], self.d_cco[:, :])
        v_ms = self.op("vector", "memset", self.kvblk[:, :], 0.0, inc="v")
        self.w("vector", "c", 64)
        self.w("vector", "gb", g2)
        self.vsync(v_ms)
        for c in range(16):
            self.op("vector", "tensor_copy",
                    self.kvblk[0:64, c * 128:c * 128 + 64],
                    self.kvks2[0:64, c * 64:(c + 1) * 64])
            self.op("vector", "tensor_copy",
                    self.kvblk[64:128, c * 128 + 64:(c + 1) * 128],
                    self.kvks2[64:128, c * 64:(c + 1) * 64])
            self.op("vector", "tensor_scalar_mul",
                    self.ksrep[:, c * 128:(c + 1) * 128],
                    self.blockmask, self.kvks2[:, 1024 + c:1025 + c],
                    inc="v" if c == 15 else None)
        self.v_kvblk = self.cnt["v"]

    def phase_b1a(self):
        self.free("ksp", "kvp", "pa")
        m, P = self.m, self.P
        self.wq_sb = [self.sb("wqs0", [128, 2048], BF16, m["out0"]),
                      self.sb("wqs1", [128, 2048], BF16, m["out0"] + 4096),
                      self.sb("wqs2", [128, 2048], BF16, m["lntt0"]),
                      self.sb("wqs3", [128, 2048], BF16, m["lntt1"])]
        self.wo_sb = self.sb("wo_sb", [128, NIC * 2048], BF16, P)
        self.q_all = self.sb("q_all", [128, NIC * 1024], BF16, P + 65536)
        self.pq = [self.ps(f"pq{i}", [128, 2048]) for i in range(2)]
        wq_slot_free = [self.t_kv_done, self.t_kv_done]
        pq_free = [0, 0]
        wo3 = self.d_wo[:, :].rearrange("(c p) d -> c p d", p=128)
        self.wo_ld = 0
        self.w("tensor", "v", self.v_kvks_evac)
        self.w("vector", "v", self.v_kvks_evac)
        for icp in range(NIC // 2):
            psl = icp % 2
            self.w("gpsimd", "t", wq_slot_free[psl])
            lds = []
            for j in range(2):
                ic = icp * 2 + j
                lds.append(self.dma(
                    "gpsimd", f"wq{psl * 2 + j}",
                    self.wq_sb[psl * 2 + j][:, :].rearrange("p (c i) -> p c i", c=NDC),
                    self.d_wq[ic, :, :, :]))
                self.wo_ld = self.dma("gpsimd", "wo",
                                      self.wo_sb[:, ic * 2048:(ic + 1) * 2048],
                                      wo3[ic, :, :])
            self.w("tensor", "v", pq_free[psl])
            for j in range(2):
                self.w("tensor", f"wq{psl * 2 + j}", lds[j])
                for dc in range(NDC):
                    for th in range(2):
                        self.op("tensor", "matmul",
                                self.pq[psl][:, j * 1024 + th * 512:j * 1024 + (th + 1) * 512],
                                self.wq_sb[psl * 2 + j][:, dc * 128:(dc + 1) * 128],
                                self.ln_fm[:, dc * 1024 + th * 512:dc * 1024 + (th + 1) * 512],
                                start=(dc == 0), stop=(dc == NDC - 1),
                                skip_group_check=True)
            wq_slot_free[psl] = self.tmark()
            self.w("vector", "t", wq_slot_free[psl])
            pq_free[psl] = self.op("vector", "tensor_scalar_max",
                                   self.q_all[:, icp * 2048:(icp + 1) * 2048],
                                   self.pq[psl][:, :], 0.0, inc="v")
        self.v_q_done = self.cnt["v"]
        self.b1a_t_done = self.cnt["t"]

    def phase_b1b(self):
        self.collective_finish()
        self.free("pq1", "pq0")
        m = self.m
        self.po = [self.ps(f"po{i}", [128, 1024]) for i in range(2)]
        self.pnb = [self.ps(f"pnb{i}", [128, 1024]) for i in range(2)]
        self.rn_sb = [self.sb("rn0", [128, 1024], BF16, m["out1"]),
                      self.sb("rn1", [128, 1024], BF16, m["out1"] + 2048)]
        self.lnx = [self.sb("lnx0", [128, 1024], F32, m["lntt0"]),
                    self.sb("lnx1", [128, 1024], F32, m["lntt1"])]
        po_free = [self.v_q_done, self.v_q_done]
        pnb_free = [self.v_q_done, self.v_q_done]
        rn_free = [0, 0]
        self.w("tensor", "v", self.v_kvblk)
        self.w("scalar", "s", self.a0_s_end)
        for ic in range(NIC):
            sl = ic % 2
            self.w("tensor", "v", po_free[sl])
            for th in range(2):
                self.op("tensor", "matmul",
                        self.po[sl][:, th * 512:(th + 1) * 512],
                        self.kvblk[:, ic * 128:(ic + 1) * 128],
                        self.q_all[:, ic * 1024 + th * 512:ic * 1024 + (th + 1) * 512],
                        start=True, stop=True)
            self.w("tensor", "v", pnb_free[sl])
            for th in range(2):
                self.op("tensor", "matmul",
                        self.pnb[sl][:, th * 512:(th + 1) * 512],
                        self.ksrep[:, ic * 128:(ic + 1) * 128],
                        self.q_all[:, ic * 1024 + th * 512:ic * 1024 + (th + 1) * 512],
                        start=True, stop=True)
            t_nb = self.tmark()
            t_qkv = t_nb
            self.w("vector", "t", t_nb)
            self.w("vector", "s", rn_free[sl])
            v_rn = self.op("vector", "tensor_scalar_max", self.rn_sb[sl][:, :],
                           self.pnb[sl][:, :], 1e-6, inc="v")
            pnb_free[sl] = v_rn
            self.w("scalar", "v", v_rn)
            s_ln = self.op("scalar", "activation", self.lnx[sl][:, :],
                           self.rn_sb[sl][:, :], AF.Ln, inc="s")
            self.w("scalar", "s", s_ln)
            s_rcp = self.op("scalar", "activation", self.rn_sb[sl][:, :],
                            self.lnx[sl][:, :], AF.Exp, scale=-1.0, inc="s")
            rn_free[sl] = s_rcp
            self.w("vector", "s", s_rcp)
            self.w("vector", "t", t_qkv)
            po_free[sl] = self.op("vector", "tensor_tensor",
                                  self.q_all[:, ic * 1024:(ic + 1) * 1024],
                                  self.po[sl][:, :], self.rn_sb[sl][:, :],
                                  ALU.mult, inc="v")
        self.v_qkvn_done = self.cnt["v"]
        self.b1b_s_done = self.cnt["s"]

    def phase_b2c(self):
        self.free("pnb1", "pnb0", "po1", "po0")
        self.pat = self.ps("pat", [128, 2048])
        self.pTc = self.ps("pTc", [128, 2048], F32)
        lnfm3 = self.ln_fm[:, :].rearrange("p (c u) -> p c u", c=NDC)
        pat_free_v = self.v_qkvn_done
        self.v_pTc = [self.v_qkvn_done, self.v_qkvn_done]
        self.tp_batch = 0
        x_free = [self.a0_x_done[-2], self.a0_x_done[-1]]
        x1w_prev = [0, 0]
        self.s_ln3 = []
        self.b2_t_done = 0
        prev_tp = None
        for tt in range(self.B2C_TTS):
            slot = tt % 2
            vv, ss = x_free[slot]
            self.w("sync", "v", vv)
            self.w("sync", "s", ss)
            vx = self.dma("sync", f"x0{slot}", self.x_sb[slot][:, :],
                          self.d_x[tt * 128:(tt + 1) * 128, :])
            self.w("tensor", "v", pat_free_v)
            if tt == 0:
                self.w("tensor", "wo", self.wo_ld)
                self.w("tensor", "gb", 48)
            for dg in range(4):
                self.op("tensor", "matmul",
                        self.pat[:, dg * 512:(dg + 1) * 512],
                        self.ones_row,
                        self.bo_sb[0:1, dg * 512:(dg + 1) * 512],
                        start=True, stop=False, skip_group_check=True)
            for ic in range(NIC):
                for dg in range(4):
                    self.op("tensor", "matmul",
                            self.pat[:, dg * 512:(dg + 1) * 512],
                            self.q_all[:, ic * 1024 + tt * 128:ic * 1024 + (tt + 1) * 128],
                            self.wo_sb[:, ic * 2048 + dg * 512:ic * 2048 + (dg + 1) * 512],
                            start=False, stop=(ic == NIC - 1), skip_group_check=True)
            t_attn = self.tmark()
            self.b2_t_done = t_attn
            if prev_tp is not None:
                pslot, ps_ap3, ptt = prev_tp
                self.w("tensor", "s", ps_ap3)
                self.transpose_batches(self.pTc, self.lntt[pslot], ptt,
                                       self.v_pTc, lnfm3)
            self.w("vector", "t", t_attn)
            if tt >= 2:
                self.w("vector", f"xw{slot}", x1w_prev[slot])
            v_cp = self.op("vector", "tensor_copy", self.x1_sb[slot][:, :],
                           self.pat[:, :], inc="v")
            pat_free_v = v_cp
            self.vsync(v_cp)
            self.w("scalar", "v", v_cp)
            v_rc2, s_sq2 = self.ln_stats(self.x1_sb[slot][:, :], slot)
            r2, nm2, s_rs2 = self.ln_rs(slot, v_rc2)
            self.w("vector", f"x0{slot}", vx)
            self.w("vector", "s", s_rs2)
            va = self.op("vector", "tensor_scalar_add", self.x_sb[slot][:, :],
                         self.x_sb[slot][:, :], nm2, inc="v")
            self.vsync(va)
            v_x1 = self.op("vector", "scalar_tensor_tensor", self.x1_sb[slot][:, :],
                           self.x1_sb[slot][:, :], r2, self.x_sb[slot][:, :],
                           ALU.mult, ALU.add, inc="v")
            self.w("sync", "v", v_x1)
            x1w_prev[slot] = self.dma("sync", f"xw{slot}",
                                      self.d_x1[tt * 128:(tt + 1) * 128, :],
                                      self.x1_sb[slot][:, :])
            self.w("scalar", "v", v_x1)
            self.vsync(v_x1)
            v_rc3, _ = self.ln_stats(self.x1_sb[slot][:, :], slot)
            r3, nm3, _ = self.ln_rs(slot, v_rc3)
            s_ap3 = self.op("scalar", "activation", self.lntt[slot][:, :],
                            self.x1_sb[slot][:, :], AF.Identity, bias=nm3, scale=r3,
                            inc="s")
            self.s_ln3.append(s_ap3)
            x_free[slot] = (v_x1, s_ap3)
            prev_tp = (slot, s_ap3, tt)
        if prev_tp is not None:
            pslot, ps_ap3, ptt = prev_tp
            self.w("tensor", "s", ps_ap3)
            self.transpose_batches(self.pTc, self.lntt[pslot], ptt,
                                   self.v_pTc, lnfm3)
        self.c_v_done = self.cnt["v"]
        self.c_s_done = self.cnt["s"]
        self.x1w_all = (self.cnt["xw0"], self.cnt["xw1"])

    def phase_d1(self):
        self.free("pTc", "pat")
        m, P = self.m, self.P
        self.pgu = [self.ps(f"pgu{i}", [128, 2048]) for i in range(2)]
        self.wg_sb = [self.sb("wgs0", [128, 2048], BF16, m["kvblk"]),
                      self.sb("wgs1", [128, 2048], BF16, m["ksrep"])]
        self.wu_sb = [self.sb("wus0", [128, 2048], BF16, P),
                      self.sb("wus1", [128, 2048], BF16, P + 4096)]
        self.gate_f = [self.sb("gf0", [128, 1024], F32, m["x0"]),
                       self.sb("gf1", [128, 1024], F32, m["x0"] + 4096)]
        self.up_f = [self.sb("uf0", [128, 1024], F32, m["x1s"]),
                     self.sb("uf1", [128, 1024], F32, m["x1s"] + 4096)]
        self.hidden = self.sb("hidden", [128, NHC * 1024], BF16, P + 8192)
        t_wg_free = [self.b2_t_done, self.b2_t_done]
        t_wu_free = [self.b2_t_done, self.b2_t_done]
        pg_free = [0, 0]
        pu_free = [0, 0]
        gfuf_free = [0, 0]
        self.w("gpsimd", "v", self.c_v_done)
        self.w("gpsimd", "s", self.c_s_done)
        self.w("tensor", "v", self.c_v_done)
        self.w("tensor", "s", self.c_s_done)
        self.w("scalar", "c", 64)
        self.w("vector", "c", 64)
        for hc in range(NHC):
            sl = hc % 2
            self.w("gpsimd", "t", t_wg_free[sl])
            ldg = self.dma("gpsimd", f"wg{sl}",
                           self.wg_sb[sl][:, :].rearrange("p (c i) -> p c i", c=NDC),
                           self.d_wg[hc, :, :, :])
            self.w("gpsimd", "t", t_wu_free[sl])
            ldu = self.dma("gpsimd", f"wu{sl}",
                           self.wu_sb[sl][:, :].rearrange("p (c i) -> p c i", c=NDC),
                           self.d_wu[hc, :, :, :])
            self.w("vector", f"wg{sl}", ldg)
            v_bg = self.op("vector", "tensor_scalar_add", self.wg_sb[sl][:, :],
                           self.wg_sb[sl][:, :], -128.0, inc="v")
            self.w("vector", f"wu{sl}", ldu)
            v_bu = self.op("vector", "tensor_scalar_add", self.wu_sb[sl][:, :],
                           self.wu_sb[sl][:, :], -128.0, inc="v")
            self.w("tensor", "v", v_bu)
            self.w("tensor", "s", pg_free[sl])
            self.w("tensor", "v", pu_free[sl])
            for dc in range(NDC):
                for th in range(2):
                    self.op("tensor", "matmul",
                            self.pgu[sl][:, th * 512:(th + 1) * 512],
                            self.wg_sb[sl][:, dc * 128:(dc + 1) * 128],
                            self.ln_fm[:, dc * 1024 + th * 512:dc * 1024 + (th + 1) * 512],
                            start=(dc == 0), stop=(dc == NDC - 1),
                            skip_group_check=True)
            for dc in range(NDC):
                for th in range(2):
                    self.op("tensor", "matmul",
                            self.pgu[sl][:, 1024 + th * 512:1024 + (th + 1) * 512],
                            self.wu_sb[sl][:, dc * 128:(dc + 1) * 128],
                            self.ln_fm[:, dc * 1024 + th * 512:dc * 1024 + (th + 1) * 512],
                            start=(dc == 0), stop=(dc == NDC - 1),
                            skip_group_check=True)
            t_u = self.tmark()
            t_wg_free[sl] = t_u
            t_wu_free[sl] = t_u
            self.w("scalar", "t", t_u)
            self.w("scalar", "v", gfuf_free[sl])
            s_si = self.op("scalar", "activation", self.gate_f[sl][:, :],
                           self.pgu[sl][:, 0:1024], AF.Silu,
                           scale=self.sg_sb[:, hc:hc + 1], inc="s")
            pg_free[sl] = s_si
            self.w("vector", "t", t_u)
            vu = self.op("vector", "tensor_scalar_mul", self.up_f[sl][:, :],
                         self.pgu[sl][:, 1024:2048], self.su_sb[:, hc:hc + 1],
                         inc="v")
            pu_free[sl] = vu
            self.w("vector", "s", s_si)
            self.vsync(vu)
            v_h = self.op("vector", "tensor_tensor",
                          self.hidden[:, hc * 1024:(hc + 1) * 1024],
                          self.gate_f[sl][:, :], self.up_f[sl][:, :],
                          ALU.mult, inc="v")
            gfuf_free[sl] = v_h
        self.v_hid_done = self.cnt["v"]
        self.d1_t_done = self.cnt["t"]
        self.d1_s_done = self.cnt["s"]

    def phase_d2(self):
        self.free("pgu1", "pgu0")
        m = self.m
        self.pd = [self.ps(f"pd{i}", [128, 1024]) for i in range(2)]
        self.wd_sb = [self.sb("wds0", [128, NHC * 128], BF16, m["out0"]),
                      self.sb("wds1", [128, NHC * 128], BF16, m["x0"])]
        self.down_fm = self.sb("down_fm", [128, NDC * 1024], BF16, m["ln_fm"])
        t_wd_free = [self.d1_t_done, self.d1_t_done]
        pd_free = [self.v_hid_done, self.v_hid_done]
        self.w("tensor", "s", self.d1_s_done)
        self.w("gpsimd", "v", self.v_hid_done)
        self.w("gpsimd", "s", self.d1_s_done)
        self.w("vector", "c", 64)
        for dc in range(NDC):
            sl = dc % 2
            self.w("gpsimd", "t", t_wd_free[sl])
            ld = self.dma("gpsimd", f"wd{sl}",
                          self.wd_sb[sl][:, :].rearrange("p (c i) -> p c i", c=NHC),
                          self.d_wd[dc, :, :, :])
            self.w("vector", f"wd{sl}", ld)
            v_b = self.op("vector", "tensor_scalar_add", self.wd_sb[sl][:, :],
                          self.wd_sb[sl][:, :], -128.0, inc="v")
            self.w("tensor", "v", v_b)
            self.w("tensor", "v", pd_free[sl])
            for hc in range(NHC):
                for th in range(2):
                    self.op("tensor", "matmul",
                            self.pd[sl][:, th * 512:(th + 1) * 512],
                            self.wd_sb[sl][:, hc * 128:(hc + 1) * 128],
                            self.hidden[:, hc * 1024 + th * 512:hc * 1024 + (th + 1) * 512],
                            start=(hc == 0), stop=(hc == NHC - 1),
                            skip_group_check=True)
            t_d = self.tmark()
            t_wd_free[sl] = t_d
            self.w("vector", "t", t_d)
            pd_free[sl] = self.op("vector", "tensor_scalar_mul",
                                  self.down_fm[:, dc * 1024:(dc + 1) * 1024],
                                  self.pd[sl][:, :], self.sd_sb[:, dc:dc + 1],
                                  inc="v")
        self.v_dfm_done = self.cnt["v"]
        self.d2_t_done = self.cnt["t"]

    def phase_d3(self):
        self.free("pd1", "pd0")
        m = self.m
        self.out_sb = [self.sb("outs0", [128, D], F32, m["out0"]),
                       self.sb("outs1", [128, D], F32, m["out1"])]
        self.pt2 = [self.ps(f"pt2{i}", [128, 2048], F32) for i in range(2)]
        pt2_free = [(self.v_dfm_done, self.d1_s_done),
                    (self.v_dfm_done, self.d1_s_done)]
        out_free = [0, 0]
        x1_slot_free = [0, 0]
        self.w("sync", "s", self.s_ln3[-1])
        self.w("sync", "xw0", self.x1w_all[0])
        self.w("sync", "xw1", self.x1w_all[1])
        y_lasts = [0, 0]
        for tt in range(NTT):
            slot = tt % 2
            if tt >= 2:
                self.w("sync", "v", x1_slot_free[slot])
            vx = self.dma("sync", f"xr{slot}", self.x1_sb[slot][:, :],
                          self.d_x1[tt * 128:(tt + 1) * 128, :])
            self.w("tensor", "v", pt2_free[slot][0])
            self.w("tensor", "s", pt2_free[slot][1])
            for dc in range(NDC):
                self.op("tensor", "matmul",
                        self.pt2[slot][:, dc * 128:(dc + 1) * 128],
                        self.down_fm[:, dc * 1024 + tt * 128:dc * 1024 + (tt + 1) * 128],
                        self.ident, start=(dc % 4 == 0), stop=True,
                        skip_group_check=True)
            t_t4 = self.tmark()
            self.w("vector", "t", t_t4)
            self.w("vector", f"y{slot}", out_free[slot])
            v_cp4 = self.op("vector", "tensor_copy", self.out_sb[slot][:, :],
                            self.pt2[slot][:, :], inc="v")
            pt2_free[slot] = (v_cp4, 0)
            self.vsync(v_cp4)
            self.w("scalar", "v", v_cp4)
            v_ve4, s_sq4 = self.ln_stats(self.out_sb[slot][:, :], slot)
            r4, nm4, _ = self.ln_rs_newton(slot, v_ve4)
            self.w("vector", f"xr{slot}", vx)
            va = self.op("vector", "tensor_scalar_add", self.x1_sb[slot][:, :],
                         self.x1_sb[slot][:, :], nm4, inc="v")
            self.vsync(va)
            v_out = self.op("vector", "scalar_tensor_tensor", self.out_sb[slot][:, :],
                            self.out_sb[slot][:, :], r4, self.x1_sb[slot][:, :],
                            ALU.mult, ALU.add, inc="v")
            x1_slot_free[slot] = v_out
            self.w("sync", "v", v_out)
            y_lasts[slot] = self.dma("sync", f"y{slot}",
                                     self.d_y[tt * 128:(tt + 1) * 128, :],
                                     self.out_sb[slot][:, :])
            out_free[slot] = y_lasts[slot]
        self.w("sync", "y0", y_lasts[0])
        self.w("sync", "y1", y_lasts[1])

    def finalize(self):
        import functools
        blk_guard = self.nc.Block()
        blk = blk_guard.__enter__()
        for eng in ENGINES:
            getattr(blk, eng)(functools.partial(_run, lst=self.ops[eng]))
        blk_guard.__exit__(None, None, None)


_BUILT = None


def _get_nc():
    global _BUILT
    if _BUILT is None:
        _BUILT = MK().nc
    return _BUILT


def kernel(L0_attn_hidden_states, L0_attn_Wq, L0_attn_Wk, L0_attn_Wv, L0_attn_Wo,
           L0_attn_bo, L0_ffn_wq_gate, L0_ffn_s_gate, L0_ffn_wq_up, L0_ffn_s_up,
           L0_ffn_wq_down, L0_ffn_s_down):
    x2d = np.ascontiguousarray(np.asarray(L0_attn_hidden_states,
                                          dtype=np.float32).reshape(S_FULL, D))
    wq = np.ascontiguousarray(np.asarray(L0_attn_Wq, np.float32)
                              .reshape(NDC, 128, NIC, 128).transpose(2, 1, 0, 3))
    wk = np.ascontiguousarray(np.asarray(L0_attn_Wk, np.float32)
                              .reshape(NDC, 128, NIG, 512).transpose(2, 1, 0, 3))
    wv = np.ascontiguousarray(np.asarray(L0_attn_Wv, np.float32)
                              .reshape(NDC, 128, NIG, 512).transpose(2, 1, 0, 3))
    wo = np.ascontiguousarray(np.asarray(L0_attn_Wo, np.float32))
    bo = np.ascontiguousarray(np.asarray(L0_attn_bo, np.float32).reshape(1, D))
    wg = np.ascontiguousarray(np.asarray(L0_ffn_wq_gate, np.int32)
                              .reshape(NHC, 128, NDC, 128).transpose(0, 3, 2, 1))
    wu = np.ascontiguousarray(np.asarray(L0_ffn_wq_up, np.int32)
                              .reshape(NHC, 128, NDC, 128).transpose(0, 3, 2, 1))
    wd = np.ascontiguousarray(np.asarray(L0_ffn_wq_down, np.int32)
                              .reshape(NDC, 128, NHC, 128).transpose(0, 3, 2, 1))
    sg = np.ascontiguousarray(np.asarray(L0_ffn_s_gate, np.float32).reshape(NHC, 128).T)
    su = np.ascontiguousarray(np.asarray(L0_ffn_s_up, np.float32).reshape(NHC, 128).T)
    sd = np.ascontiguousarray(np.asarray(L0_ffn_s_down, np.float32).reshape(NDC, 128).T)
    cst = np.zeros((128, 512), dtype=ml_dtypes.bfloat16)
    cst[:, 0:128] = np.eye(128, dtype=ml_dtypes.bfloat16)
    cst[0:64, 128:192] = 1
    cst[64:128, 192:256] = 1
    cst[:, 256] = 1
    cst[:, 258:386] = 1

    nc = _get_nc()
    in_maps = []
    for c in range(NCORES):
        in_maps.append({
            "x": np.ascontiguousarray(x2d[c * T:(c + 1) * T]),
            "wq": wq, "wk": wk, "wv": wv, "wo": wo, "bo": bo,
            "wg": wg, "wu": wu, "wd": wd, "sg": sg, "su": su, "sd": sd,
            "cst": cst,
        })
    res = run_bass_kernel_spmd(nc, in_maps, core_ids=list(range(NCORES)))
    out = np.concatenate([res.results[c]["y"] for c in range(NCORES)], axis=0)
    return out.reshape(1, S_FULL, D).astype(np.float32)
